# version 19
# baseline (speedup 1.0000x reference)
import numpy as np
from contextlib import ExitStack

import concourse.bass as bass
import concourse.mybir as mybir
from concourse.bass_utils import run_bass_kernel_spmd

F32 = mybir.dt.float32
BF16 = mybir.dt.bfloat16
AF = mybir.ActivationFunctionType
ALU = mybir.AluOpType
AX = mybir.AxisListType

D = 1024
NE = 8
NHS = 8
EXP_SHIFT = -20.0
LN_EPS = 1e-5
QK_SCALE = 0.125

ENGS = ("sync", "tensor", "scalar", "vector", "gpsimd")


class P:
    def __init__(self, nc, es):
        self.nc = nc
        self.es = es
        self.ops = {e: [] for e in ENGS}
        self.sems = {}
        self.cnt = {}
        self.dma_sems = set()

    def sem(self, name):
        if name not in self.sems:
            self.sems[name] = self.es.enter_context(self.nc.semaphore(name))
            self.cnt[name] = 0
        return name

    def op(self, eng, fn, waits=(), inc=None, k=1):
        w = [(s, v) for (s, v) in waits if v and v > 0]
        rec = [fn, w, None]
        if inc is not None:
            self.sem(inc)
            self.cnt[inc] += k
            rec[2] = (inc, k)
        self.ops[eng].append(rec)
        return self.cnt[inc] if inc is not None else None

    def now(self, s):
        return self.cnt.get(s, 0)

    def barrier(self):
        marks = {}
        for e, s in (("tensor", "pe"), ("scalar", "act"), ("vector", "dve")):
            self.op(e, None, waits=[(s, self.cnt[s])])
            self.op(e, lambda eng: eng.nop(), inc=s)
            marks[s] = self.cnt[s]
        for s in sorted(self.dma_sems):
            marks[s] = self.cnt[s]
        wl = [(s, v) for s, v in marks.items() if v > 0]
        for e in ENGS:
            self.op(e, None, waits=wl)

    def emit(self, blk):
        bodies = {"sync": blk.sync, "tensor": blk.tensor, "scalar": blk.scalar,
                  "vector": blk.vector, "gpsimd": blk.gpsimd}
        sems = self.sems
        for ename in ENGS:
            ops = self.ops[ename]

            def make_body(ops=ops):
                def body(eng):
                    cur = {}
                    for fn, waits, inc in ops:
                        for (s, v) in waits:
                            if cur.get(s, 0) < v:
                                eng.wait_ge(sems[s], v)
                                cur[s] = v
                        if fn is not None:
                            instr = fn(eng)
                            if inc is not None:
                                instr.then_inc(sems[inc[0]], inc[1])
                return body
            bodies[ename](make_body())


def build_program(S=4096, upto="D"):
    NB = S // 128
    NJ = NB // 2
    NP = NJ // 2
    TOK = NJ * 128
    THALF = TOK // 2
    NT256 = THALF // 256
    NTH = THALF // 128

    nc = bass.Bass("TRN2", target_bir_lowering=False)
    es = ExitStack()
    p = P(nc, es)
    for s in ("pe", "act", "dve"):
        p.sem(s)

    xh = nc.dram_tensor("x_core", [S, D], F32, kind="ExternalInput").ap()
    w_attn = {n: nc.dram_tensor(n, [D, D], F32, kind="ExternalInput").ap()
              for n in ("wq", "wk", "wv", "wo")}
    w1_h = nc.dram_tensor("w1", [NE, D, D], F32, kind="ExternalInput").ap()
    w2_h = nc.dram_tensor("w2", [NE, D, D], F32, kind="ExternalInput").ap()
    CF = 2384
    cf_h = nc.dram_tensor("consts_f32", [128, CF], F32, kind="ExternalInput").ap()
    cb_h = nc.dram_tensor("consts_bf", [128, 256], BF16, kind="ExternalInput").ap()
    flg_h = nc.dram_tensor("flags_col", [128, NJ], F32, kind="ExternalInput").ap()
    out_h = nc.dram_tensor("out", [TOK, D], F32, kind="ExternalOutput").ap()

    TOPLIM = 229300

    def at(name, shape, dtype, off):
        nbytes = int(np.prod(shape[1:])) * mybir.dt.size(dtype)
        assert off + nbytes <= TOPLIM, f"sbuf overflow {name}: {off}+{nbytes}"
        return nc.alloc_sbuf_tensor_at(name, shape, dtype, offset=off)

    class Bump:
        def __init__(self, base):
            self.o = base

        def a(self, name, shape, dtype, n=1):
            if n > 1:
                return [self.a(f"{name}{i}", shape, dtype) for i in range(n)]
            t = at(name, shape, dtype, self.o)
            nb = int(np.prod(shape[1:])) * mybir.dt.size(dtype)
            self.o += ((nb + 31) // 32) * 32
            return t

    pr = Bump(16640)
    cf32 = pr.a("cf32", [128, CF], F32)
    cbf = pr.a("cbf", [128, 256], BF16)
    flgc = pr.a("flgc", [128, NJ], F32)
    idf = cf32[:, 0:128]
    ones = cf32[:, 128:192]
    gbbc = cf32[:, 192:200]
    b1c = cf32[:, 200:264]
    gwsb = cf32[:, 264:328]
    scc = cf32[:, 328:330]
    selsb = cf32[0:8, 336:1360]
    b2nat = cf32[0:8, 1360:2384]
    idb = cbf[:, 0:128]
    tri = cbf[:, 128:256]
    gwbf = pr.a("gwbf", [128, 8 * NE], BF16)
    g_tm = pr.a("g_tm", [128, NJ * D], BF16)
    ao = pr.a("ao", [128, 4 * TOK], BF16, n=2)
    OV = pr.o
    assert OV <= 93000, OV

    ab = Bump(OV)
    xn_own = ab.a("xn_own", [128, 8 * THALF], BF16)
    combT = ab.a("combT", [8, TOK], F32)
    AB0 = ab.o
    asc = Bump(AB0)
    xtileA = asc.a("xtileA", [128, D], F32, n=2)
    xnbfA = asc.a("xnbfA", [128, D], BF16, n=2)
    xnf32A = asc.a("xnf32A", [128, D], F32, n=2)
    xff = asc.a("xff", [128, 8 * 128], F32, n=2)
    stA = asc.a("stA", [128, 32], F32, n=2)
    gscr = asc.a("gscr", [128, 48], F32, n=2)
    bsc = Bump(AB0)
    w1sb = bsc.a("w1sb", [128, 8 * D], BF16, n=2)
    w2sb = bsc.a("w2sb", [128, 8 * D], BF16, n=2)
    g_acc = bsc.a("g_acc", [128, NTH * D], F32)
    hrel = bsc.a("hrel", [128, 256], F32, n=2)
    htil = bsc.a("htil", [128, 4 * 256], BF16)
    assert bsc.o <= TOPLIM and asc.o <= TOPLIM

    cb = Bump(OV)
    K_sb = cb.a("K_sb", [128, 4 * S], BF16)
    V_sb = cb.a("V_sb", [128, NB * NHS * 65], BF16)
    Q_sb = cb.a("Q_sb", [128, 4 * TOK], BF16)
    wkT = cb.a("wkT", [128, 8 * 512], BF16)
    wvT = cb.a("wvT", [128, 8 * 512], BF16)
    wqT = cb.a("wqT", [128, 8 * 512], BF16)
    C0 = cb.o
    xtileC = cb.a("xtileC", [128, D], F32, n=2)
    wnatC = [xtileC[i][:].bitcast(BF16)[:, 0:D] for i in range(2)]
    xnbfC = cb.a("xnbfC", [128, D], BF16, n=2)
    stC = cb.a("stC", [128, 32], F32, n=2)
    xnc = cb.a("xnc", [128, 8 * 512], BF16, n=2)
    assert cb.o <= TOPLIM, cb.o
    c2b = Bump(C0 + 4096)
    probs = c2b.a("probs", [128, 3 * 512], BF16, n=3)
    araw = c2b.a("araw", [128, 512], F32, n=2)
    rsc = c2b.a("rsc", [128, 512], F32)
    qz = c2b.a("qz", [128, 512], BF16, n=2)
    maskq = c2b.a("maskq", [128, 512], BF16, n=7)
    assert c2b.o <= cb.o, (c2b.o, cb.o)

    dbm = Bump(OV)
    woT = dbm.a("woT", [128, 8 * D], BF16)
    wnatD = dbm.a("wnatD", [128, D], BF16, n=2)
    xresD = dbm.a("xresD", [128, D], F32, n=2)
    outsb = dbm.a("outsb", [128, D], F32, n=2)
    assert dbm.o <= TOPLIM, dbm.o

    psAB = nc.alloc_psum_tensor("psAB", [128, 3072], F32)
    psA = psAB[:, 0:1536]
    psB = psAB[:, 1536:3072]
    psC = nc.alloc_psum_tensor("psC", [128, 512], F32)
    psD = nc.alloc_psum_tensor("psD", [128, 512], F32)

    def dma(eng, dst, src, sem, waits=()):
        p.sem(sem)
        p.dma_sems.add(sem)
        return p.op(eng, lambda e, d=dst, s=src: e.dma_start(d, s),
                    waits=waits, inc=sem, k=16)

    c_f = dma("gpsimd", cf32[:], cf_h, "cs0")
    c_b = dma("gpsimd", cbf[:], cb_h, "cs1")
    c_g = dma("gpsimd", flgc[:], flg_h, "cs2")
    dma("gpsimd", gwbf[:], cf_h[:, 264:328], "sw_gwbf")
    cw = {"idb": 0, "idf": 0, "tri": 0, "ones": 0, "flg": 0, "gb": 0,
          "b1": 0, "gw": 0, "b2": 0, "sel": 0, "scc": 0}
    CWS = {"idb": ("cs1", c_b), "tri": ("cs1", c_b), "flg": ("cs2", c_g)}
    def cws(key):
        return CWS.get(key, ("cs0", c_f))

    def plan_A(half, ntiles=None):
        trbf = [psA[:, 0:512].bitcast(BF16), psB[:, 0:512].bitcast(BF16)]
        trf = [psA[:, 512:1536], psB[:, 512:1536]]
        x_free = [0, 0]
        st_act = [0, 0]
        xnbf_free = [0, 0]
        trbf_free = [0, 0]
        gate_free = 0
        comb_free = 0
        st_free = [0, 0]
        gsc_free = [0, 0]
        jts = list(range(half * (NJ // 2), (half + 1) * (NJ // 2)))
        if ntiles is not None:
            jts = jts[:ntiles]
        nt = len(jts)
        gsc_all = xff[0]
        sc2 = xff[1]
        for idx, jt in enumerate(jts):
            par = idx % 2
            tl = idx * 128
            xw = dma("sync", xtileA[par][:], xh[jt * 256: jt * 256 + 128, :],
                     f"sx{par}", waits=[("act", x_free[par])])
            st = stA[par]
            c1 = p.op("vector", lambda e, a=st[:, 0:6], b=xtileA[par][:, 0:512]:
                      e.bn_stats(a, b), waits=[(f"sx{par}", xw), ("dve", st_free[par]),
                                               ("act", st_act[par])],
                      inc="dve")
            c2 = p.op("vector", lambda e, a=st[:, 6:12], b=xtileA[par][:, 512:1024]:
                      e.bn_stats(a, b), waits=[("dve", c1)], inc="dve")
            c3 = p.op("vector", lambda e, a=st[:, 12:14], b=st[:, 0:12]: e.bn_aggr(a, b),
                      waits=[("dve", c2)], inc="dve")
            a1 = p.op("scalar", lambda e, a=st[:, 14:15], b=st[:, 13:14]:
                      e.activation(a, b, AF.Sqrt, bias=scc[:, 0:1], scale=1.0),
                      waits=[("dve", c3), cws("scc")], inc="act")
            c4 = p.op("vector", lambda e, a=st[:, 15:16], b=st[:, 14:15]: e.reciprocal(a, b),
                      waits=[("act", a1)], inc="dve")
            c4b = p.op("vector", lambda e, o=st[:, 16:17], i=st[:, 12:13], t=st[:, 15:16]:
                       e.scalar_tensor_tensor(o, i, -1.0, t, ALU.mult, ALU.mult),
                       waits=[("dve", c4)], inc="dve")
            c5 = p.op("scalar", lambda e, o=xnbfA[par][:], i=xtileA[par][:],
                      b=st[:, 16:17], r=st[:, 15:16]:
                      e.activation(o, i, AF.Identity, bias=b, scale=r),
                      waits=[("dve", c4b), ("pe", xnbf_free[par])], inc="act")
            x_free[par] = c5
            st_act[par] = c5
            st_free[par] = c4b
            tw = None
            for dc in range(8):
                tw = p.op("tensor", lambda e, o=trbf[par][:, dc * 128:(dc + 1) * 128],
                          i=xnbfA[par][:, dc * 128:(dc + 1) * 128], n=idb[:]:
                          e.transpose(o, i, n),
                          waits=[("act", c5), ("act", trbf_free[par]), cws("idb")],
                          inc="pe")
            xnbf_free[par] = tw
            o3 = xn_own.rearrange("p (c t) -> p c t", t=THALF)[:, :, tl:tl + 128]
            i3 = trbf[par][:, 0:1024].rearrange("p (c t) -> p c t", t=128)
            cpw = p.op("scalar", lambda e, o=o3, i=i3: e.copy(o, i),
                       waits=[("pe", tw)], inc="act")
            trbf_free[par] = cpw
            gw8 = None
            for dc in range(8):
                gw8 = p.op("tensor", lambda e, o=psC[:, 0:8],
                           l=xn_own[:, dc * THALF + tl: dc * THALF + tl + 128],
                           r=gwbf[:, dc * 8:(dc + 1) * 8], s0=(dc == 0), s1=(dc == 7):
                           e.matmul(o, l, r, start=s0, stop=s1),
                           waits=[("act", cpw), ("sw_gwbf", 16), ("dve", gate_free)],
                           inc="pe")
            gate_free = p.op("vector", lambda e, o=gsc_all[:, idx * 8:(idx + 1) * 8],
                             a=psC[:, 0:8], b=gbbc[:]: e.tensor_add(o, a, b),
                             waits=[("pe", gw8), cws("gb")], inc="dve")

        def seg(ap2):
            return ap2.rearrange("p (t u) -> p t u", u=8)

        def col1(ap2):
            return ap2.rearrange("p (t u) -> p t u", u=1).broadcast_to([128, nt, 8])

        n8 = nt * 8
        gv = gsc_all[:, 0:n8]
        m1 = sc2[:, 0:nt]
        cnt = sc2[:, 32:32 + nt]
        m2a = sc2[:, 64:64 + nt]
        t1_ = sc2[:, 96:96 + nt]
        t2_ = sc2[:, 112:112 + nt]
        m2 = sc2[:, 128:128 + nt]
        eqm = sc2[:, 256:256 + n8]
        sp_ = sc2[:, 384:384 + n8]
        msk = sc2[:, 512:512 + n8]
        cmb = sc2[:, 640:640 + n8]
        d = p.op("vector", lambda e, o=m1.rearrange("p (t u) -> p t u", u=1), i=seg(gv):
                 e.reduce_max(o, i, axis=AX.X), waits=[("dve", gate_free)], inc="dve")
        d = p.op("vector", lambda e, o=seg(eqm), a=seg(gv), b=col1(m1):
                 e.tensor_tensor(o, a, b, ALU.is_equal), waits=[("dve", d)], inc="dve")
        d = p.op("vector", lambda e, o=cnt.rearrange("p (t u) -> p t u", u=1), i=seg(eqm):
                 e.reduce_sum(o, i, axis=AX.X), waits=[("dve", d)], inc="dve")
        d = p.op("vector", lambda e, o=sp_, i=eqm, t=gv:
                 e.scalar_tensor_tensor(o, i, -1e30, t, ALU.mult, ALU.add),
                 waits=[("dve", d)], inc="dve")
        d = p.op("vector", lambda e, o=m2a.rearrange("p (t u) -> p t u", u=1), i=seg(sp_):
                 e.reduce_max(o, i, axis=AX.X), waits=[("dve", d)], inc="dve")
        d = p.op("vector", lambda e, o=t1_, a=m1, b=m2a: e.tensor_sub(o, a, b),
                 waits=[("dve", d)], inc="dve")
        d = p.op("vector", lambda e, o=t2_, i=cnt:
                 e.tensor_single_scalar(o, i, 1.5, ALU.is_ge), waits=[("dve", d)], inc="dve")
        d = p.op("vector", lambda e, o=t2_, a=t2_, b=t1_: e.tensor_mul(o, a, b),
                 waits=[("dve", d)], inc="dve")
        d = p.op("vector", lambda e, o=m2, a=m2a, b=t2_: e.tensor_add(o, a, b),
                 waits=[("dve", d)], inc="dve")
        d = p.op("vector", lambda e, o=seg(msk), a=seg(gv), b=col1(m2):
                 e.tensor_tensor(o, a, b, ALU.is_ge), waits=[("dve", d)], inc="dve")
        d = p.op("vector", lambda e, o=cmb, a=gv, b=msk: e.tensor_mul(o, a, b),
                 waits=[("dve", d)], inc="dve")
        tslot_free = [0, 0]
        for idx, jt in enumerate(jts):
            sl = idx % 2
            t3 = p.op("tensor", lambda e, o=psD[0:8, sl * 128:(sl + 1) * 128],
                      i=cmb[:, idx * 8:(idx + 1) * 8], n=idf[:]: e.transpose(o, i, n),
                      waits=[("dve", d), ("dve", tslot_free[sl]), cws("idf")], inc="pe")
            cpt = p.op("vector", lambda e, o=combT[0:8, jt * 128:(jt + 1) * 128],
                       i=psD[0:8, sl * 128:(sl + 1) * 128]: e.tensor_copy(o, i),
                       waits=[("pe", t3)], inc="dve")
            tslot_free[sl] = cpt

    def plan_B(half):
        h_ps = [psA[:, 0:256], psA[:, 512:768]]
        gq = [psB[:, 0:512], psB[:, 512:1024], psB[:, 1024:1536], psA[:, 1024:1536]]
        cbc = [psC, psD]
        w_free = [0, 0]
        wload = {}

        def w_ap(wh, e_):
            return wh[e_].rearrange("(c p) h -> p c h", p=128)

        def load_unit(e_):
            parw = e_ % 2
            a_ = dma("gpsimd", w1sb[parw][:].rearrange("p (c h) -> p c h", c=8),
                     w_ap(w1_h, e_), f"w1{parw}", waits=[("pe", w_free[parw])])
            b_ = dma("gpsimd", w2sb[parw][:].rearrange("p (c h) -> p c h", c=8),
                     w_ap(w2_h, e_), f"w2{parw}", waits=[("pe", w_free[parw])])
            wload[e_] = (parw, a_, b_)

        load_unit(0)
        load_unit(1)
        h_free = [0, 0]
        hrel_free = [0, 0]
        htil_free = [0, 0, 0, 0]
        gps_free = [0, 0, 0, 0]
        cbc_free = [0, 0]
        gacc_rw = {}
        nbu = 0
        for e_ in range(NE):
            parw = e_ % 2
            for tt in range(NT256):
                cpar = nbu % 2
                nbu += 1
                gtok = half * THALF + tt * 256
                bc = p.op("tensor", lambda e, o=cbc[cpar][:, 0:256],
                          l=selsb[0:8, e_ * 128:(e_ + 1) * 128],
                          r=combT[0:8, gtok: gtok + 256]:
                          e.matmul(o, l, r, start=True, stop=True),
                          waits=[cws("sel"), ("dve", cbc_free[cpar])], inc="pe")
                pend_f2 = None
                lastmult = 0
                for hdc in range(8):
                    php = (tt * 8 + hdc) % 2
                    slot = hdc % 4
                    mm = None
                    for dc in range(8):
                        mm = p.op("tensor", lambda e, o=h_ps[php],
                                  l=w1sb[parw][:, dc * D + hdc * 128: dc * D + (hdc + 1) * 128],
                                  r=xn_own[:, dc * THALF + tt * 256: dc * THALF + (tt + 1) * 256],
                                  s0=(dc == 0), s1=(dc == 7):
                                  e.matmul(o, l, r, start=s0, stop=s1),
                                  waits=[(f"w1{wload[e_][0]}", wload[e_][1]),
                                         ("act", h_free[php])], inc="pe")
                    if pend_f2 is not None:
                        pend_f2()
                        pend_f2 = None
                    ar = p.op("scalar", lambda e, o=hrel[php][:], i=h_ps[php],
                              b=b1c[:, e_ * 8 + hdc: e_ * 8 + hdc + 1]:
                              e.activation(o, i, AF.Relu, bias=b, scale=1.0),
                              waits=[("pe", mm), cws("b1"), ("dve", hrel_free[php])],
                              inc="act")
                    h_free[php] = ar
                    ml = p.op("vector", lambda e, o=htil[:, slot * 256:(slot + 1) * 256],
                              a=hrel[php][:], b=cbc[cpar][:, 0:256]: e.tensor_mul(o, a, b),
                              waits=[("act", ar), ("pe", bc), ("pe", htil_free[slot])],
                              inc="dve")
                    hrel_free[php] = ml
                    lastmult = ml

                    def do_f2(hdc=hdc, slot=slot, ml=ml, e_=e_, tt=tt, parw=parw):
                        last = None
                        for tsub in range(2):
                            for dh in range(2):
                                q = tsub * 2 + dh
                                if hdc == 0 and e_ == 0:
                                    t0 = half * THALF // 128 + tt * 2
                                    p.op("tensor", lambda e, o=gq[q],
                                         l=combT[0:8, (t0 + tsub) * 128:(t0 + tsub + 1) * 128],
                                         r=b2nat[0:8, dh * 512:(dh + 1) * 512]:
                                         e.matmul(o, l, r, start=True, stop=False),
                                         waits=[cws("b2"), ("dve", gps_free[q])],
                                         inc="pe")
                                last = p.op("tensor", lambda e, o=gq[q],
                                            l=htil[:, slot * 256 + tsub * 128:
                                                   slot * 256 + (tsub + 1) * 128],
                                            r=w2sb[parw][:, hdc * D + dh * 512:
                                                          hdc * D + (dh + 1) * 512],
                                            s0=(hdc == 0 and e_ != 0), s1=(hdc == 7):
                                            e.matmul(o, l, r, start=s0, stop=s1),
                                            waits=[("dve", ml),
                                                   (f"w2{wload[e_][0]}", wload[e_][2]),
                                                   ("dve", gps_free[q] if hdc == 0 else 0)],
                                            inc="pe")
                        htil_free[slot] = last
                        return last
                    pend_f2 = do_f2
                f2last = pend_f2()
                cbc_free[cpar] = lastmult
                for tsub in range(2):
                    for dh in range(2):
                        q = tsub * 2 + dh
                        tloc = tt * 2 + tsub
                        tgt = g_acc[:, tloc * D + dh * 512: tloc * D + (dh + 1) * 512]
                        keyp = (tloc, dh)
                        if e_ == 0:
                            ev = p.op("vector", lambda e, o=tgt, a=gq[q]: e.tensor_copy(o, a),
                                      waits=[("pe", f2last), ("dve", gacc_rw.get(keyp, 0))],
                                      inc="dve")
                        else:
                            ev = p.op("vector", lambda e, o=tgt, a=gq[q], b=tgt:
                                      e.tensor_add(o, a, b),
                                      waits=[("pe", f2last), ("dve", gacc_rw.get(keyp, 0))],
                                      inc="dve")
                        gacc_rw[keyp] = ev
                        gps_free[q] = ev
            w_free[parw] = p.now("pe")
            if e_ + 2 < NE:
                load_unit(e_ + 2)
        for tloc in range(NTH):
            gt = half * NTH + tloc
            p.op("scalar", lambda e, o=g_tm[:, gt * D:(gt + 1) * D],
                 i=g_acc[:, tloc * D:(tloc + 1) * D]: e.activation(o, i, AF.Silu),
                 waits=[("dve", max(gacc_rw.get((tloc, 0), 0), gacc_rw.get((tloc, 1), 0)))],
                 inc="act")

    def plan_C1(sweep):
        trbf = [psA[:, 0:512].bitcast(BF16), psB[:, 0:512].bitcast(BF16)]
        stage_free = [0, 0]
        ps_free = [0, 0]
        wt_done = {}
        for which, dst in (("wk", wkT), ("wv", wvT), ("wq", wqT)):
            last = 0
            for ec in range(4):
                sp = ec % 2
                wv_ = dma("gpsimd", wnatC[sp],
                          w_attn[which][sweep * 512 + ec * 128: sweep * 512 + (ec + 1) * 128, :],
                          f"wn{sp}", waits=[("pe", stage_free[sp])])
                tw = None
                for hc in range(8):
                    tw = p.op("tensor", lambda e, o=trbf[sp][:, hc * 128:(hc + 1) * 128],
                              i=wnatC[sp][:, hc * 128:(hc + 1) * 128], n=idb[:]:
                              e.transpose(o, i, n),
                              waits=[(f"wn{sp}", wv_), ("dve", ps_free[sp]), cws("idb")],
                              inc="pe")
                stage_free[sp] = tw
                cp = None
                for hc in range(8):
                    cp = p.op("vector",
                              lambda e, o=dst[:, hc * 512 + ec * 128: hc * 512 + (ec + 1) * 128],
                              i=trbf[sp][:, hc * 128:(hc + 1) * 128]: e.tensor_copy(o, i),
                              waits=[("pe", tw)], inc="dve")
                ps_free[sp] = cp
                last = cp
            wt_done[which] = last

        ones_done = 0
        for c in range(NB):
            ones_done = p.op("vector", lambda e, o=V_sb[:, c * NHS * 65:(c + 1) * NHS * 65]
                             .rearrange("p (h u) -> p h u", h=NHS)[:, :, 64:65]:
                             e.memset(o, 1.0), inc="dve")

        K_ps = [psA[:, 0:512], psA[:, 512:1024]]
        V_ps = [psB[:, 0:512], psB[:, 512:1024]]
        trx = [psA[:, 1024:1536].bitcast(BF16), psB[:, 1024:1536].bitcast(BF16)]
        Q_ps = [psC, psD]
        x_free = [0, 0]
        st_free = [0, 0]
        xnbf_free = [0, 0]
        trx_free = [0, 0]
        kps_free = [0, 0]
        vps_free = [0, 0]
        qps_free = [0, 0]
        xnc_free = [0, 0]
        grp_ready = {}
        xw_extra = [[("pe", stage_free[0])], [("pe", stage_free[1])]]

        def stage_stream(grp):
            gpar = grp % 2
            cp = None
            for ti in range(4):
                tk = grp * 4 + ti
                par = tk % 2
                xw = dma("sync", xtileC[par][:], xh[tk * 128:(tk + 1) * 128, :],
                         f"sx{par}", waits=[("dve", x_free[par])] + xw_extra[par])
                xw_extra[par] = []
                st = stC[par]
                c1 = p.op("vector", lambda e, a=st[:, 0:6], b=xtileC[par][:, 0:512]:
                          e.bn_stats(a, b), waits=[(f"sx{par}", xw), ("dve", st_free[par])],
                          inc="dve")
                c2 = p.op("vector", lambda e, a=st[:, 6:12], b=xtileC[par][:, 512:1024]:
                          e.bn_stats(a, b), waits=[("dve", c1)], inc="dve")
                c3 = p.op("vector", lambda e, a=st[:, 12:14], b=st[:, 0:12]: e.bn_aggr(a, b),
                          waits=[("dve", c2)], inc="dve")
                a1 = p.op("scalar", lambda e, a=st[:, 14:15], b=st[:, 13:14]:
                          e.activation(a, b, AF.Sqrt, bias=scc[:, 0:1], scale=1.0),
                          waits=[("dve", c3), cws("scc")], inc="act")
                c4 = p.op("vector", lambda e, a=st[:, 15:16], b=st[:, 14:15]:
                          e.reciprocal(a, b), waits=[("act", a1)], inc="dve")
                c5 = p.op("vector", lambda e, o=xnbfC[par][:], i=xtileC[par][:],
                          m=st[:, 12:13], r=st[:, 15:16]:
                          e.tensor_scalar(o, i, m, r, ALU.subtract, ALU.mult),
                          waits=[("dve", c4), ("pe", xnbf_free[par])], inc="dve")
                x_free[par] = c5
                st_free[par] = c5
                tw = None
                for hc in range(8):
                    tw = p.op("tensor", lambda e, o=trx[par][:, hc * 128:(hc + 1) * 128],
                              i=xnbfC[par][:, hc * 128:(hc + 1) * 128], n=idb[:]:
                              e.transpose(o, i, n),
                              waits=[("dve", c5), ("act", trx_free[par])], inc="pe")
                xnbf_free[par] = tw
                for hc in range(8):
                    cp = p.op("scalar",
                              lambda e, o=xnc[gpar][:, hc * 512 + ti * 128:
                                                    hc * 512 + (ti + 1) * 128],
                              i=trx[par][:, hc * 128:(hc + 1) * 128]: e.copy(o, i),
                              waits=[("pe", tw), ("pe", xnc_free[gpar])], inc="act")
                trx_free[par] = cp
            grp_ready[grp] = cp

        def stage_mms(grp):
            gpar = grp % 2
            gr = grp_ready[grp]
            kv_last = 0
            for ec in range(4):
                kp = ec % 2
                mm = None
                for hc in range(8):
                    mm = p.op("tensor", lambda e, o=K_ps[kp],
                              l=wkT[:, hc * 512 + ec * 128: hc * 512 + (ec + 1) * 128],
                              r=xnc[gpar][:, hc * 512:(hc + 1) * 512],
                              s0=(hc == 0), s1=(hc == 7): e.matmul(o, l, r, start=s0, stop=s1),
                              waits=[("act", gr), ("dve", wt_done["wk"]),
                                     ("act", kps_free[kp])], inc="pe")
                kv_last = mm
                kps_free[kp] = p.op("scalar", lambda e,
                                    o=K_sb[:, ec * S + grp * 512:(ec * S + (grp + 1) * 512)],
                                    i=K_ps[kp]: e.activation(o, i, AF.Copy),
                                    waits=[("pe", mm)], inc="act")
            for ti in range(4):
                vp = ti % 2
                tk = grp * 4 + ti
                mm = None
                for hc in range(8):
                    mm = p.op("tensor", lambda e, o=V_ps[vp],
                              l=xnc[gpar][:, hc * 512 + ti * 128: hc * 512 + (ti + 1) * 128],
                              r=wvT[:, hc * 512:(hc + 1) * 512],
                              s0=(hc == 0), s1=(hc == 7): e.matmul(o, l, r, start=s0, stop=s1),
                              waits=[("act", gr), ("dve", wt_done["wv"]),
                                     ("act", vps_free[vp])], inc="pe")
                kv_last = mm
                vps_free[vp] = p.op("scalar", lambda e,
                                    o=V_sb[:, tk * NHS * 65:(tk + 1) * NHS * 65].rearrange(
                                        "p (h u) -> p h u", h=NHS)[:, :, 0:64],
                                    i=V_ps[vp].rearrange("p (h u) -> p h u", u=64),
                                    : e.activation(o, i, AF.Copy),
                                    waits=[("pe", mm)], inc="act")
            for ti in (0, 2):
                jt = 2 * grp + ti // 2
                for ec in range(4):
                    qp = ec % 2
                    mm = None
                    for hc in range(8):
                        mm = p.op("tensor", lambda e, o=Q_ps[qp][:, 0:128],
                                  l=wqT[:, hc * 512 + ec * 128: hc * 512 + (ec + 1) * 128],
                                  r=xnc[gpar][:, hc * 512 + ti * 128: hc * 512 + (ti + 1) * 128],
                                  s0=(hc == 0), s1=(hc == 7):
                                  e.matmul(o, l, r, start=s0, stop=s1),
                                  waits=[("act", gr), ("dve", wt_done["wq"]),
                                         ("act", qps_free[qp])], inc="pe")
                    kv_last = mm
                    qps_free[qp] = p.op("scalar", lambda e,
                                        o=Q_sb[:, ec * TOK + jt * 128: ec * TOK + (jt + 1) * 128],
                                        i=Q_ps[qp][:, 0:128]: e.activation(o, i, AF.Copy),
                                        waits=[("pe", mm)], inc="act")
            xnc_free[gpar] = kv_last

        NG = NB // 4
        for g in range(NG + 1):
            if g >= 1:
                stage_mms(g - 1)
            if g < NG:
                stage_stream(g)


    def plan_C2(sweep):
        NPQ = NP // 2
        sc_ps = [psA, psB]
        pv = psC
        bc = psD
        G = []
        for iq in range(NPQ):
            for h in range(NHS):
                chunks = list(range(8 * iq + 8))
                gs = [chunks[k:k + 3] for k in range(0, len(chunks), 3)]
                for gi, ch in enumerate(gs):
                    G.append((iq, h, ch, gi == 0, gi == len(gs) - 1))
        exp_done2 = [0, 0]
        probs_rd_pe = [0, 0, 0]
        probs_rd_dve = [0, 0, 0]
        araw_free = [0, 0]
        pv_reader_act = 0
        pv_reader_dve = 0
        bc_reader = 0
        qz_free = [0, 0]
        qz_ready = [0, 0]
        zinit = [0, 0]
        pend = []
        pend2 = []

        def head_slices(h):
            return h // 2, (h % 2) * 64

        for hp in range(2):
            zinit[hp] = p.op("vector",
                             lambda e, o=qz[hp][(1 - hp) * 64:(1 - hp) * 64 + 64, 0:512]:
                             e.memset(o, 0.0), inc="dve")
        mk_done = 0
        for r_ in range(1, 8):
            m_ = maskq[r_ - 1]
            mk_done = p.op("vector", lambda e, o=m_[:]: e.memset(o, 1.0),
                           waits=[("dve", mk_done)], inc="dve")
            for jl in range(4):
                rel = r_ - 2 * jl
                sl = (jl * 128, (jl + 1) * 128)
                if rel < 0:
                    continue
                elif rel == 0:
                    mk_done = p.op("vector", lambda e, o=m_[:, sl[0]:sl[1]], i=tri[:]:
                                   e.tensor_copy(o, i),
                                   waits=[("dve", mk_done), cws("tri")], inc="dve")
                elif rel == 1:
                    mk_done = p.op("vector", lambda e, o=m_[:, sl[0]:sl[1]],
                                   i=m_[:, sl[0]:sl[1]], s=flgc[:, jl:jl + 1]:
                                   e.tensor_single_scalar(o, i, s, ALU.mult),
                                   waits=[("dve", mk_done), cws("flg")], inc="dve")
                else:
                    mk_done = p.op("vector", lambda e, o=m_[:, sl[0]:sl[1]]:
                                   e.memset(o, 0.0), waits=[("dve", mk_done)], inc="dve")

        def plan_pv(item):
            nonlocal pv_reader_act, pv_reader_dve
            (t, iq, h, ch, first, last, par3, ed, md) = item
            pv_last = None
            for k, c in enumerate(ch):
                w = [("act", ed)]
                if md:
                    w.append(("dve", md))
                if first and k == 0:
                    w += [("dve", pv_reader_dve), ("act", pv_reader_act)]
                pv_last = p.op("tensor", lambda e, o=pv[0:65, 0:512],
                               l=V_sb[:, c * NHS * 65 + h * 65: c * NHS * 65 + (h + 1) * 65],
                               r=probs[par3][:, k * 512:(k + 1) * 512],
                               s0=(first and k == 0), s1=(last and k == len(ch) - 1):
                               e.matmul(o, l, r, start=s0, stop=s1), waits=w, inc="pe")
            probs_rd_pe[par3] = pv_last
            probs_rd_dve[par3] = md
            if last:
                apar = (iq * NHS + h) % 2
                rc = p.op("vector", lambda e, o=rsc[64:65, 0:512], i=pv[64:65, 0:512]:
                          e.reciprocal(o, i), waits=[("pe", pv_last)], inc="dve")
                acp = p.op("vector", lambda e, o=araw[apar][0:64, 0:512], i=pv[0:64, 0:512]:
                           e.tensor_copy(o, i),
                           waits=[("pe", pv_last), ("dve", araw_free[apar])], inc="dve")
                pv_reader_act = 0
                pv_reader_dve = acp

                def flush(iq=iq, h=h, rc=rc, acp=acp, apar=apar):
                    nonlocal bc_reader
                    ecq, b0q = head_slices(h)
                    bcw = p.op("tensor", lambda e, o=bc[0:64, 0:512],
                               l=ones[64:65, 0:64], r=rsc[64:65, 0:512]:
                               e.matmul(o, l, r, start=True, stop=True),
                               waits=[("dve", rc), cws("ones"), ("dve", bc_reader)],
                               inc="pe")
                    mb = p.op("vector", lambda e,
                              o=ao[sweep][b0q:b0q + 64,
                                          ecq * TOK + iq * 512: ecq * TOK + (iq + 1) * 512],
                              a=araw[apar][0:64, 0:512], b=bc[0:64, 0:512]:
                              e.tensor_mul(o, a, b),
                              waits=[("pe", bcw), ("dve", acp)], inc="dve")
                    bc_reader = mb
                    araw_free[apar] = mb
                pend.append(flush)

        prev = None
        prev2 = None
        last_qz = {}
        for t, (iq, h, ch, first, last) in enumerate(G):
            par = t % 2
            par3 = t % 3
            ec, b0 = head_slices(h)
            hp = h % 2
            if first:
                last_qz[(iq, h)] = p.op(
                    "vector", lambda e, o=qz[hp][b0:b0 + 64, 0:512],
                    i=Q_sb[b0:b0 + 64, ec * TOK + iq * 512: ec * TOK + (iq + 1) * 512]:
                    e.tensor_copy(o, i),
                    waits=[("pe", qz_free[hp]), ("dve", zinit[hp])], inc="dve")
                qz_ready[hp] = last_qz[(iq, h)]
            sc_last = None
            for k, c in enumerate(ch):
                sc_last = p.op("tensor", lambda e,
                               o=sc_ps[par][:, k * 512:(k + 1) * 512],
                               l=K_sb[:, ec * S + c * 128: ec * S + (c + 1) * 128],
                               r=qz[hp][:, 0:512]:
                               e.matmul(o, l, r, start=True, stop=True),
                               waits=[("act", exp_done2[par]), ("dve", qz_ready[hp])],
                               inc="pe")
            if last:
                qz_free[hp] = sc_last
            ed = p.op("scalar", lambda e, o=probs[par3][:, 0:len(ch) * 512],
                      i=sc_ps[par][:, 0:len(ch) * 512]:
                      e.activation(o, i, AF.Exp, bias=scc[:, 1:2], scale=QK_SCALE),
                      waits=[("pe", sc_last), cws("scc"),
                             ("pe", probs_rd_pe[par3]), ("dve", probs_rd_dve[par3])],
                      inc="act")
            exp_done2[par] = ed
            md = 0
            for k, c in enumerate(ch):
                r_ = c - 8 * iq
                if r_ < 0:
                    continue
                if r_ == 0:
                    md = p.op("vector", lambda e, o=probs[par3][:, k * 512: k * 512 + 128],
                              a=probs[par3][:, k * 512: k * 512 + 128], b=tri[:]:
                              e.tensor_mul(o, a, b),
                              waits=[("act", ed), cws("tri")], inc="dve")
                else:
                    md = p.op("vector", lambda e, o=probs[par3][:, k * 512:(k + 1) * 512],
                              a=probs[par3][:, k * 512:(k + 1) * 512], b=maskq[r_ - 1][:]:
                              e.tensor_mul(o, a, b),
                              waits=[("act", ed), ("dve", mk_done)], inc="dve")
            if prev2 is not None:
                plan_pv(prev2)
            for fn in pend2:
                fn()
            pend2 = pend
            pend = []
            prev2 = prev
            prev = (t, iq, h, ch, first, last, par3, ed, md)
        for item in (prev2, prev):
            if item is not None:
                plan_pv(item)
        for fn in pend2 + pend:
            fn()

    def plan_D():
        trbf = [psA[:, 0:512].bitcast(BF16), psB[:, 0:512].bitcast(BF16)]
        stage_free = [0, 0]
        ps_free = [0, 0]
        wo_done = 0
        for dc in range(8):
            sp = dc % 2
            wv_ = dma("gpsimd", wnatD[sp][:], w_attn["wo"][dc * 128:(dc + 1) * 128, :],
                      f"wn{sp}", waits=[("pe", stage_free[sp])])
            tw = None
            for ec in range(8):
                tw = p.op("tensor", lambda e, o=trbf[sp][:, (ec % 4) * 128:((ec % 4) + 1) * 128],
                          i=wnatD[sp][:, ec * 128:(ec + 1) * 128], n=idb[:]:
                          e.transpose(o, i, n),
                          waits=[(f"wn{sp}", wv_), ("dve", ps_free[sp]), cws("idb")], inc="pe")
                if ec % 4 == 3:
                    cp = None
                    for e4 in range(4):
                        ecg = ec - 3 + e4
                        cp = p.op("vector", lambda e,
                                  o=woT[:, (ecg) * D + dc * 128: (ecg) * D + (dc + 1) * 128],
                                  i=trbf[sp][:, e4 * 128:(e4 + 1) * 128]: e.tensor_copy(o, i),
                                  waits=[("pe", tw)], inc="dve")
                    ps_free[sp] = cp
                    wo_done = cp
            stage_free[sp] = tw
        s_ps = [psA[:, 512:1024], psB[:, 512:1024], psA[:, 0:512], psB[:, 0:512]]
        x_free = [0, 0]
        outdma = [0, 0]
        sp_free = [0, 0, 0, 0]
        outw = []
        pend = []
        for jt in range(NJ):
            par = jt % 2
            xw = dma("sync", xresD[par][:], xh[jt * 256: jt * 256 + 128, :],
                     f"sx{par}", waits=[("dve", x_free[par])])
            mm_of = {}
            for dh in range(2):
                pp = (jt % 2) * 2 + dh
                mm = None
                for ecg in range(8):
                    sweep, lc = ecg // 4, ecg % 4
                    mm = p.op("tensor", lambda e, o=s_ps[pp],
                              l=ao[sweep][:, lc * TOK + jt * 128: lc * TOK + (jt + 1) * 128],
                              r=woT[:, ecg * D + dh * 512: ecg * D + (dh + 1) * 512],
                              s0=(ecg == 0), s1=(ecg == 7): e.matmul(o, l, r, start=s0, stop=s1),
                              waits=[("dve", wo_done), ("dve", sp_free[pp])], inc="pe")
                mm_of[dh] = (pp, mm)
            for fn in pend:
                fn()
            pend = []

            def mk_ep(jt=jt, par=par, mm_of=mm_of, xw=xw):
                def do_ep():
                    lw = 0
                    for dh in range(2):
                        pp, mm = mm_of[dh]
                        t1 = p.op("vector", lambda e, o=outsb[par][:, dh * 512:(dh + 1) * 512],
                                  a=s_ps[pp], b=g_tm[:, jt * D + dh * 512: jt * D + (dh + 1) * 512]:
                                  e.tensor_mul(o, a, b),
                                  waits=[("pe", mm), (f"so{par}", outdma[par])], inc="dve")
                        lw = p.op("vector", lambda e, o=outsb[par][:, dh * 512:(dh + 1) * 512],
                                  a=outsb[par][:, dh * 512:(dh + 1) * 512],
                                  b=xresD[par][:, dh * 512:(dh + 1) * 512]:
                                  e.tensor_add(o, a, b),
                                  waits=[("dve", t1), (f"sx{par}", xw)], inc="dve")
                        sp_free[pp] = lw
                    x_free[par] = lw
                    ow = dma("sync", out_h[jt * 128:(jt + 1) * 128, :], outsb[par][:],
                             f"so{par}", waits=[("dve", lw)])
                    outdma[par] = ow
                    outw.append((par, ow, lw))
                return do_ep
            pend.append(mk_ep())
        for fn in pend:
            fn()
        return outw

    phases = [("A0", lambda: plan_A(0)), ("B0", lambda: plan_B(0)),
              ("A1t1", lambda: plan_A(1, 1)), ("A1t2", lambda: plan_A(1, 2)),
              ("A1t3", lambda: plan_A(1, 3)),
              ("A1", lambda: plan_A(1)), ("B1", lambda: plan_B(1)),
              ("C01", lambda: plan_C1(0)), ("C02", lambda: plan_C2(0)),
              ("C11", lambda: plan_C1(1)), ("C12", lambda: plan_C2(1)),
              ("D", lambda: plan_D())]
    for name, fn in phases:
        if name.startswith("A1t") and not upto.startswith("A1t"):
            continue
        fn()
        p.barrier()
        if name == upto:
            break
    p.op("sync", None, waits=[("so0", p.now("so0")), ("so1", p.now("so1"))])

    with nc.Block() as blk:
        p.emit(blk)
    return nc, es


def _layout(S=4096):
    NB = S // 128
    NJ = NB // 2
    cores = []
    for r in range(2):
        own, loc, flags = [], [], []
        for j in range(NJ):
            pj = (j % 2) if r == 0 else 1 - (j % 2)
            o = 2 * j + pj
            t = 2 * j + (1 - pj)
            own.append(o)
            loc += [o, t]
            flags.append(1.0 if pj == 1 else 0.0)
        cores.append((own, loc, flags))
    return cores


def _consts(gate_b, b1, gate_w):
    import ml_dtypes
    NEl = 8
    cf = np.zeros((128, 2384), np.float32)
    cf[:, 0:128] = np.eye(128, dtype=np.float32)
    cf[:, 128:192] = 1.0
    cf[:, 192:200] = np.broadcast_to(np.asarray(gate_b, np.float32)[None, :], (128, 8))
    cf[:, 200:264] = np.asarray(b1, np.float32).reshape(8, 8, 128)         .transpose(2, 0, 1).reshape(128, 64)
    cf[:, 264:328] = np.asarray(gate_w, np.float32).reshape(8, 128, 8)         .transpose(1, 0, 2).reshape(128, 64)
    cf[:, 328] = LN_EPS
    cf[:, 329] = EXP_SHIFT
    cf[0:8, 336:1360] = np.repeat(np.eye(8, dtype=np.float32)[:, :, None], 128, axis=2)         .reshape(8, 8 * 128)
    cb = np.zeros((128, 256), ml_dtypes.bfloat16)
    cb[:, 0:128] = np.eye(128, dtype=ml_dtypes.bfloat16)
    cb[:, 128:256] = np.tril(np.ones((128, 128), np.float32)).T.astype(ml_dtypes.bfloat16)
    return cf, cb


def kernel(L0_attn_x, L0_attn_wq, L0_attn_wk, L0_attn_wv, L0_attn_wo,
           L0_ffn_gate_w, L0_ffn_gate_b, L0_ffn_w1, L0_ffn_b1, L0_ffn_w2,
           L0_ffn_b2):
    x = np.ascontiguousarray(np.asarray(L0_attn_x, dtype=np.float32))
    B, S, _ = x.shape
    NJ = S // 256
    nc, es = build_program(S)
    cores = _layout(S)

    cf, cb = _consts(L0_ffn_gate_b, L0_ffn_b1, L0_ffn_gate_w)
    cf[0:8, 1360:2384] = np.asarray(L0_ffn_b2, np.float32)
    shared = {
        "wq": np.ascontiguousarray(np.asarray(L0_attn_wq, np.float32)),
        "wk": np.ascontiguousarray(np.asarray(L0_attn_wk, np.float32)),
        "wv": np.ascontiguousarray(np.asarray(L0_attn_wv, np.float32)),
        "wo": np.ascontiguousarray(np.asarray(L0_attn_wo, np.float32)),
        "w1": np.ascontiguousarray(np.asarray(L0_ffn_w1, np.float32)),
        "w2": np.ascontiguousarray(np.asarray(L0_ffn_w2, np.float32)),
        "consts_f32": cf,
        "consts_bf": cb,
    }
    in_maps = []
    metas = []
    for b in range(B):
        for r in range(2):
            own, loc, flags = cores[r]
            xb = x[b].reshape(S // 128, 128, 1024)
            x_core = np.ascontiguousarray(xb[loc].reshape(S, 1024))
            flags_col = np.ascontiguousarray(
                np.broadcast_to(np.asarray(flags, np.float32)[None, :], (128, NJ)))
            m = dict(shared)
            m["x_core"] = x_core
            m["flags_col"] = flags_col
            in_maps.append(m)
            metas.append((b, own))
    res = run_bass_kernel_spmd(nc, in_maps, core_ids=list(range(2 * B)))
    out = np.empty((B, S, 1024), np.float32)
    for ci, (b, own) in enumerate(metas):
        oc = np.asarray(res.results[ci]["out"]).reshape(NJ, 128, 1024)
        ob = out[b].reshape(S // 128, 128, 1024)
        for j, o in enumerate(own):
            ob[o] = oc[j]
    es.close()
    return out


# version 20 (final)
# speedup vs baseline: 1.2181x; 1.2181x over previous
import numpy as np
from contextlib import ExitStack

import concourse.bass as bass
import concourse.mybir as mybir
from concourse.bass_utils import run_bass_kernel_spmd

F32 = mybir.dt.float32
BF16 = mybir.dt.bfloat16
AF = mybir.ActivationFunctionType
ALU = mybir.AluOpType
AX = mybir.AxisListType

D = 1024
NE = 8
NHS = 8
EXP_SHIFT = -20.0
LN_EPS = 1e-5
QK_SCALE = 0.125

ENGS = ("sync", "tensor", "scalar", "vector", "gpsimd")


class P:
    def __init__(self, nc, es):
        self.nc = nc
        self.es = es
        self.ops = {e: [] for e in ENGS}
        self.sems = {}
        self.cnt = {}
        self.dma_sems = set()

    def sem(self, name):
        if name not in self.sems:
            self.sems[name] = self.es.enter_context(self.nc.semaphore(name))
            self.cnt[name] = 0
        return name

    def op(self, eng, fn, waits=(), inc=None, k=1):
        w = [(s, v) for (s, v) in waits if v and v > 0]
        rec = [fn, w, None]
        if inc is not None:
            self.sem(inc)
            self.cnt[inc] += k
            rec[2] = (inc, k)
        self.ops[eng].append(rec)
        return self.cnt[inc] if inc is not None else None

    def now(self, s):
        return self.cnt.get(s, 0)

    def barrier(self):
        marks = {}
        for e, s in (("tensor", "pe"), ("scalar", "act"), ("vector", "dve")):
            self.op(e, None, waits=[(s, self.cnt[s])])
            self.op(e, lambda eng: eng.nop(), inc=s)
            marks[s] = self.cnt[s]
        for s in sorted(self.dma_sems):
            marks[s] = self.cnt[s]
        wl = [(s, v) for s, v in marks.items() if v > 0]
        for e in ENGS:
            self.op(e, None, waits=wl)

    def emit(self, blk):
        bodies = {"sync": blk.sync, "tensor": blk.tensor, "scalar": blk.scalar,
                  "vector": blk.vector, "gpsimd": blk.gpsimd}
        sems = self.sems
        for ename in ENGS:
            ops = self.ops[ename]

            def make_body(ops=ops):
                def body(eng):
                    cur = {}
                    for fn, waits, inc in ops:
                        for (s, v) in waits:
                            if cur.get(s, 0) < v:
                                eng.wait_ge(sems[s], v)
                                cur[s] = v
                        if fn is not None:
                            instr = fn(eng)
                            if inc is not None:
                                instr.then_inc(sems[inc[0]], inc[1])
                return body
            bodies[ename](make_body())


def build_program(S=4096, upto="D"):
    NB = S // 128
    NJ = NB // 2
    NP = NJ // 2
    TOK = NJ * 128
    THALF = TOK // 2
    NT256 = THALF // 256
    NTH = THALF // 128

    nc = bass.Bass("TRN2", target_bir_lowering=False)
    es = ExitStack()
    p = P(nc, es)
    for s in ("pe", "act", "dve"):
        p.sem(s)

    xh = nc.dram_tensor("x_core", [S, D], F32, kind="ExternalInput").ap()
    w_attn = {n: nc.dram_tensor(n, [D, D], F32, kind="ExternalInput").ap()
              for n in ("wq", "wk", "wv", "wo")}
    w1_h = nc.dram_tensor("w1", [NE, D, D], F32, kind="ExternalInput").ap()
    w2_h = nc.dram_tensor("w2", [NE, D, D], F32, kind="ExternalInput").ap()
    CF = 2384
    cf_h = nc.dram_tensor("consts_f32", [128, CF], F32, kind="ExternalInput").ap()
    cb_h = nc.dram_tensor("consts_bf", [128, 256], BF16, kind="ExternalInput").ap()
    flg_h = nc.dram_tensor("flags_col", [128, NJ], F32, kind="ExternalInput").ap()
    out_h = nc.dram_tensor("out", [TOK, D], F32, kind="ExternalOutput").ap()

    TOPLIM = 229300

    def at(name, shape, dtype, off):
        nbytes = int(np.prod(shape[1:])) * mybir.dt.size(dtype)
        assert off + nbytes <= TOPLIM, f"sbuf overflow {name}: {off}+{nbytes}"
        return nc.alloc_sbuf_tensor_at(name, shape, dtype, offset=off)

    class Bump:
        def __init__(self, base):
            self.o = base

        def a(self, name, shape, dtype, n=1):
            if n > 1:
                return [self.a(f"{name}{i}", shape, dtype) for i in range(n)]
            t = at(name, shape, dtype, self.o)
            nb = int(np.prod(shape[1:])) * mybir.dt.size(dtype)
            self.o += ((nb + 31) // 32) * 32
            return t

    pr = Bump(16640)
    cf32 = pr.a("cf32", [128, CF], F32)
    cbf = pr.a("cbf", [128, 256], BF16)
    flgc = pr.a("flgc", [128, NJ], F32)
    idf = cf32[:, 0:128]
    ones = cf32[:, 128:192]
    gbbc = cf32[:, 192:200]
    b1c = cf32[:, 200:264]
    gwsb = cf32[:, 264:328]
    scc = cf32[:, 328:330]
    selsb = cf32[0:8, 336:1360]
    b2nat = cf32[0:8, 1360:2384]
    idb = cbf[:, 0:128]
    tri = cbf[:, 128:256]
    gwbf = pr.a("gwbf", [128, 8 * NE], BF16)
    g_tm = pr.a("g_tm", [128, NJ * D], BF16)
    ao = pr.a("ao", [128, 4 * TOK], BF16, n=2)
    OV = pr.o
    assert OV <= 93000, OV

    ab = Bump(OV)
    xn_own = ab.a("xn_own", [128, 8 * THALF], BF16)
    combT = ab.a("combT", [8, TOK], F32)
    AB0 = ab.o
    asc = Bump(AB0)
    xtileA = asc.a("xtileA", [128, D], F32, n=2)
    xnbfA = asc.a("xnbfA", [128, D], BF16, n=2)
    xnf32A = asc.a("xnf32A", [128, D], F32, n=2)
    xff = asc.a("xff", [128, 8 * 128], F32, n=2)
    stA = asc.a("stA", [128, 32], F32, n=2)
    gscr = asc.a("gscr", [128, 48], F32, n=2)
    bsc = Bump(AB0)
    w1sb = bsc.a("w1sb", [128, 8 * D], BF16, n=2)
    w2sb = bsc.a("w2sb", [128, 8 * D], BF16, n=2)
    g_acc = bsc.a("g_acc", [128, NTH * D], F32)
    hrel = bsc.a("hrel", [128, 256], F32, n=2)
    htil = bsc.a("htil", [128, 4 * 256], BF16)
    assert bsc.o <= TOPLIM and asc.o <= TOPLIM

    cb = Bump(OV)
    K_sb = cb.a("K_sb", [128, 4 * S], BF16)
    V_sb = cb.a("V_sb", [128, NB * NHS * 65], BF16)
    Q_sb = cb.a("Q_sb", [128, 4 * TOK], BF16)
    wkT = cb.a("wkT", [128, 8 * 512], BF16)
    wvT = cb.a("wvT", [128, 8 * 512], BF16)
    wqT = cb.a("wqT", [128, 8 * 512], BF16)
    C0 = cb.o
    xtileC = cb.a("xtileC", [128, D], F32, n=2)
    wnatC = [xtileC[i][:].bitcast(BF16)[:, 0:D] for i in range(2)]
    xnbfC = cb.a("xnbfC", [128, D], BF16, n=2)
    stC = cb.a("stC", [128, 32], F32, n=2)
    xnc = cb.a("xnc", [128, 8 * 512], BF16, n=2)
    assert cb.o <= TOPLIM, cb.o
    c2b = Bump(C0 + 4096)
    probs = c2b.a("probs", [128, 3 * 512], BF16, n=3)
    araw = c2b.a("araw", [128, 512], F32, n=2)
    rsc = c2b.a("rsc", [128, 512], F32)
    qz = c2b.a("qz", [128, 512], BF16, n=2)
    maskq = c2b.a("maskq", [128, 512], BF16, n=7)
    assert c2b.o <= cb.o, (c2b.o, cb.o)

    dbm = Bump(OV)
    woT = dbm.a("woT", [128, 8 * D], BF16)
    wnatD = dbm.a("wnatD", [128, D], BF16, n=2)
    xresD = dbm.a("xresD", [128, D], F32, n=2)
    outsb = dbm.a("outsb", [128, D], F32, n=2)
    assert dbm.o <= TOPLIM, dbm.o

    psAB = nc.alloc_psum_tensor("psAB", [128, 3072], F32)
    psA = psAB[:, 0:1536]
    psB = psAB[:, 1536:3072]
    psC = nc.alloc_psum_tensor("psC", [128, 512], F32)
    psD = nc.alloc_psum_tensor("psD", [128, 512], F32)

    def dma(eng, dst, src, sem, waits=()):
        p.sem(sem)
        p.dma_sems.add(sem)
        return p.op(eng, lambda e, d=dst, s=src: e.dma_start(d, s),
                    waits=waits, inc=sem, k=16)

    c_f = dma("gpsimd", cf32[:], cf_h, "cs0")
    c_b = dma("gpsimd", cbf[:], cb_h, "cs1")
    c_g = dma("gpsimd", flgc[:], flg_h, "cs2")
    dma("gpsimd", gwbf[:], cf_h[:, 264:328], "sw_gwbf")
    cw = {"idb": 0, "idf": 0, "tri": 0, "ones": 0, "flg": 0, "gb": 0,
          "b1": 0, "gw": 0, "b2": 0, "sel": 0, "scc": 0}
    CWS = {"idb": ("cs1", c_b), "tri": ("cs1", c_b), "flg": ("cs2", c_g)}
    def cws(key):
        return CWS.get(key, ("cs0", c_f))

    def plan_A(half, ntiles=None):
        trbf = [psA[:, 0:512].bitcast(BF16), psB[:, 0:512].bitcast(BF16)]
        trf = [psA[:, 512:1536], psB[:, 512:1536]]
        x_free = [0, 0]
        st_act = [0, 0]
        xnbf_free = [0, 0]
        trbf_free = [0, 0]
        gate_free = 0
        comb_free = 0
        st_free = [0, 0]
        gsc_free = [0, 0]
        jts = list(range(half * (NJ // 2), (half + 1) * (NJ // 2)))
        if ntiles is not None:
            jts = jts[:ntiles]
        nt = len(jts)
        gsc_all = xff[0]
        sc2 = xff[1]
        for idx, jt in enumerate(jts):
            par = idx % 2
            tl = idx * 128
            xw = dma("sync", xtileA[par][:], xh[jt * 256: jt * 256 + 128, :],
                     f"sx{par}", waits=[("act", x_free[par])])
            st = stA[par]
            c1 = p.op("vector", lambda e, a=st[:, 0:6], b=xtileA[par][:, 0:512]:
                      e.bn_stats(a, b), waits=[(f"sx{par}", xw), ("dve", st_free[par]),
                                               ("act", st_act[par])],
                      inc="dve")
            c2 = p.op("vector", lambda e, a=st[:, 6:12], b=xtileA[par][:, 512:1024]:
                      e.bn_stats(a, b), waits=[("dve", c1)], inc="dve")
            c3 = p.op("vector", lambda e, a=st[:, 12:14], b=st[:, 0:12]: e.bn_aggr(a, b),
                      waits=[("dve", c2)], inc="dve")
            a1 = p.op("scalar", lambda e, a=st[:, 14:15], b=st[:, 13:14]:
                      e.activation(a, b, AF.Sqrt, bias=scc[:, 0:1], scale=1.0),
                      waits=[("dve", c3), cws("scc")], inc="act")
            c4 = p.op("vector", lambda e, a=st[:, 15:16], b=st[:, 14:15]: e.reciprocal(a, b),
                      waits=[("act", a1)], inc="dve")
            c4b = p.op("vector", lambda e, o=st[:, 16:17], i=st[:, 12:13], t=st[:, 15:16]:
                       e.scalar_tensor_tensor(o, i, -1.0, t, ALU.mult, ALU.mult),
                       waits=[("dve", c4)], inc="dve")
            c5 = p.op("scalar", lambda e, o=xnbfA[par][:], i=xtileA[par][:],
                      b=st[:, 16:17], r=st[:, 15:16]:
                      e.activation(o, i, AF.Identity, bias=b, scale=r),
                      waits=[("dve", c4b), ("pe", xnbf_free[par])], inc="act")
            x_free[par] = c5
            st_act[par] = c5
            st_free[par] = c4b
            tw = None
            for dc in range(8):
                tw = p.op("tensor", lambda e, o=trbf[par][:, dc * 128:(dc + 1) * 128],
                          i=xnbfA[par][:, dc * 128:(dc + 1) * 128], n=idb[:]:
                          e.transpose(o, i, n),
                          waits=[("act", c5), ("act", trbf_free[par]), cws("idb")],
                          inc="pe")
            xnbf_free[par] = tw
            o3 = xn_own.rearrange("p (c t) -> p c t", t=THALF)[:, :, tl:tl + 128]
            i3 = trbf[par][:, 0:1024].rearrange("p (c t) -> p c t", t=128)
            cpw = p.op("scalar", lambda e, o=o3, i=i3: e.copy(o, i),
                       waits=[("pe", tw)], inc="act")
            trbf_free[par] = cpw
            gw8 = None
            for dc in range(8):
                gw8 = p.op("tensor", lambda e, o=psC[:, 0:8],
                           l=xn_own[:, dc * THALF + tl: dc * THALF + tl + 128],
                           r=gwbf[:, dc * 8:(dc + 1) * 8], s0=(dc == 0), s1=(dc == 7):
                           e.matmul(o, l, r, start=s0, stop=s1),
                           waits=[("act", cpw), ("sw_gwbf", 16), ("dve", gate_free)],
                           inc="pe")
            gate_free = p.op("vector", lambda e, o=gsc_all[:, idx * 8:(idx + 1) * 8],
                             a=psC[:, 0:8], b=gbbc[:]: e.tensor_add(o, a, b),
                             waits=[("pe", gw8), cws("gb")], inc="dve")

        def seg(ap2):
            return ap2.rearrange("p (t u) -> p t u", u=8)

        def col1(ap2):
            return ap2.rearrange("p (t u) -> p t u", u=1).broadcast_to([128, nt, 8])

        n8 = nt * 8
        gv = gsc_all[:, 0:n8]
        m1 = sc2[:, 0:nt]
        cnt = sc2[:, 32:32 + nt]
        m2a = sc2[:, 64:64 + nt]
        t1_ = sc2[:, 96:96 + nt]
        t2_ = sc2[:, 112:112 + nt]
        m2 = sc2[:, 128:128 + nt]
        eqm = sc2[:, 256:256 + n8]
        sp_ = sc2[:, 384:384 + n8]
        msk = sc2[:, 512:512 + n8]
        cmb = sc2[:, 640:640 + n8]
        d = p.op("vector", lambda e, o=m1.rearrange("p (t u) -> p t u", u=1), i=seg(gv):
                 e.reduce_max(o, i, axis=AX.X), waits=[("dve", gate_free)], inc="dve")
        d = p.op("vector", lambda e, o=seg(eqm), a=seg(gv), b=col1(m1):
                 e.tensor_tensor(o, a, b, ALU.is_equal), waits=[("dve", d)], inc="dve")
        d = p.op("vector", lambda e, o=cnt.rearrange("p (t u) -> p t u", u=1), i=seg(eqm):
                 e.reduce_sum(o, i, axis=AX.X), waits=[("dve", d)], inc="dve")
        d = p.op("vector", lambda e, o=sp_, i=eqm, t=gv:
                 e.scalar_tensor_tensor(o, i, -1e30, t, ALU.mult, ALU.add),
                 waits=[("dve", d)], inc="dve")
        d = p.op("vector", lambda e, o=m2a.rearrange("p (t u) -> p t u", u=1), i=seg(sp_):
                 e.reduce_max(o, i, axis=AX.X), waits=[("dve", d)], inc="dve")
        d = p.op("vector", lambda e, o=t1_, a=m1, b=m2a: e.tensor_sub(o, a, b),
                 waits=[("dve", d)], inc="dve")
        d = p.op("vector", lambda e, o=t2_, i=cnt:
                 e.tensor_single_scalar(o, i, 1.5, ALU.is_ge), waits=[("dve", d)], inc="dve")
        d = p.op("vector", lambda e, o=t2_, a=t2_, b=t1_: e.tensor_mul(o, a, b),
                 waits=[("dve", d)], inc="dve")
        d = p.op("vector", lambda e, o=m2, a=m2a, b=t2_: e.tensor_add(o, a, b),
                 waits=[("dve", d)], inc="dve")
        d = p.op("vector", lambda e, o=seg(msk), a=seg(gv), b=col1(m2):
                 e.tensor_tensor(o, a, b, ALU.is_ge), waits=[("dve", d)], inc="dve")
        d = p.op("vector", lambda e, o=cmb, a=gv, b=msk: e.tensor_mul(o, a, b),
                 waits=[("dve", d)], inc="dve")
        tslot_free = [0, 0]
        for idx, jt in enumerate(jts):
            sl = idx % 2
            t3 = p.op("tensor", lambda e, o=psD[0:8, sl * 128:(sl + 1) * 128],
                      i=cmb[:, idx * 8:(idx + 1) * 8], n=idf[:]: e.transpose(o, i, n),
                      waits=[("dve", d), ("dve", tslot_free[sl]), cws("idf")], inc="pe")
            cpt = p.op("vector", lambda e, o=combT[0:8, jt * 128:(jt + 1) * 128],
                       i=psD[0:8, sl * 128:(sl + 1) * 128]: e.tensor_copy(o, i),
                       waits=[("pe", t3)], inc="dve")
            tslot_free[sl] = cpt

    def plan_B(half):
        h_ps = [psA[:, 0:256], psA[:, 512:768]]
        gq = [psB[:, 0:512], psB[:, 512:1024], psB[:, 1024:1536], psA[:, 1024:1536]]
        cbc = [psC, psD]
        w_free = [0, 0]
        wload = {}

        def w_ap(wh, e_):
            return wh[e_].rearrange("(c p) h -> p c h", p=128)

        def load_unit(e_):
            parw = e_ % 2
            a_ = dma("gpsimd", w1sb[parw][:].rearrange("p (c h) -> p c h", c=8),
                     w_ap(w1_h, e_), f"w1{parw}", waits=[("pe", w_free[parw])])
            b_ = dma("gpsimd", w2sb[parw][:].rearrange("p (c h) -> p c h", c=8),
                     w_ap(w2_h, e_), f"w2{parw}", waits=[("pe", w_free[parw])])
            wload[e_] = (parw, a_, b_)

        load_unit(0)
        load_unit(1)
        h_free = [0, 0]
        hrel_free = [0, 0]
        htil_free = [0, 0, 0, 0]
        gps_free = [0, 0, 0, 0]
        cbc_free = [0, 0]
        gacc_rw = {}
        nbu = 0
        for e_ in range(NE):
            parw = e_ % 2
            for tt in range(NT256):
                cpar = nbu % 2
                nbu += 1
                gtok = half * THALF + tt * 256
                bc = p.op("tensor", lambda e, o=cbc[cpar][:, 0:256],
                          l=selsb[0:8, e_ * 128:(e_ + 1) * 128],
                          r=combT[0:8, gtok: gtok + 256]:
                          e.matmul(o, l, r, start=True, stop=True),
                          waits=[cws("sel"), ("dve", cbc_free[cpar])], inc="pe")
                pend_f2 = None
                lastmult = 0
                for hdc in range(8):
                    php = (tt * 8 + hdc) % 2
                    slot = hdc % 4
                    mm = None
                    for dc in range(8):
                        mm = p.op("tensor", lambda e, o=h_ps[php],
                                  l=w1sb[parw][:, dc * D + hdc * 128: dc * D + (hdc + 1) * 128],
                                  r=xn_own[:, dc * THALF + tt * 256: dc * THALF + (tt + 1) * 256],
                                  s0=(dc == 0), s1=(dc == 7):
                                  e.matmul(o, l, r, start=s0, stop=s1),
                                  waits=[(f"w1{wload[e_][0]}", wload[e_][1]),
                                         ("act", h_free[php])], inc="pe")
                    if pend_f2 is not None:
                        pend_f2()
                        pend_f2 = None
                    ar = p.op("scalar", lambda e, o=hrel[php][:], i=h_ps[php],
                              b=b1c[:, e_ * 8 + hdc: e_ * 8 + hdc + 1]:
                              e.activation(o, i, AF.Relu, bias=b, scale=1.0),
                              waits=[("pe", mm), cws("b1"), ("dve", hrel_free[php])],
                              inc="act")
                    h_free[php] = ar
                    ml = p.op("vector", lambda e, o=htil[:, slot * 256:(slot + 1) * 256],
                              a=hrel[php][:], b=cbc[cpar][:, 0:256]: e.tensor_mul(o, a, b),
                              waits=[("act", ar), ("pe", bc), ("pe", htil_free[slot])],
                              inc="dve")
                    hrel_free[php] = ml
                    lastmult = ml

                    def do_f2(hdc=hdc, slot=slot, ml=ml, e_=e_, tt=tt, parw=parw):
                        last = None
                        for tsub in range(2):
                            for dh in range(2):
                                q = tsub * 2 + dh
                                if hdc == 0 and e_ == 0:
                                    t0 = half * THALF // 128 + tt * 2
                                    p.op("tensor", lambda e, o=gq[q],
                                         l=combT[0:8, (t0 + tsub) * 128:(t0 + tsub + 1) * 128],
                                         r=b2nat[0:8, dh * 512:(dh + 1) * 512]:
                                         e.matmul(o, l, r, start=True, stop=False),
                                         waits=[cws("b2"), ("dve", gps_free[q])],
                                         inc="pe")
                                last = p.op("tensor", lambda e, o=gq[q],
                                            l=htil[:, slot * 256 + tsub * 128:
                                                   slot * 256 + (tsub + 1) * 128],
                                            r=w2sb[parw][:, hdc * D + dh * 512:
                                                          hdc * D + (dh + 1) * 512],
                                            s0=(hdc == 0 and e_ != 0), s1=(hdc == 7):
                                            e.matmul(o, l, r, start=s0, stop=s1),
                                            waits=[("dve", ml),
                                                   (f"w2{wload[e_][0]}", wload[e_][2]),
                                                   ("dve", gps_free[q] if hdc == 0 else 0)],
                                            inc="pe")
                        htil_free[slot] = last
                        return last
                    pend_f2 = do_f2
                f2last = pend_f2()
                cbc_free[cpar] = lastmult
                for tsub in range(2):
                    for dh in range(2):
                        q = tsub * 2 + dh
                        tloc = tt * 2 + tsub
                        tgt = g_acc[:, tloc * D + dh * 512: tloc * D + (dh + 1) * 512]
                        keyp = (tloc, dh)
                        if e_ == 0:
                            ev = p.op("vector", lambda e, o=tgt, a=gq[q]: e.tensor_copy(o, a),
                                      waits=[("pe", f2last), ("dve", gacc_rw.get(keyp, 0))],
                                      inc="dve")
                        else:
                            ev = p.op("vector", lambda e, o=tgt, a=gq[q], b=tgt:
                                      e.tensor_add(o, a, b),
                                      waits=[("pe", f2last), ("dve", gacc_rw.get(keyp, 0))],
                                      inc="dve")
                        gacc_rw[keyp] = ev
                        gps_free[q] = ev
            w_free[parw] = p.now("pe")
            if e_ + 2 < NE:
                load_unit(e_ + 2)
        for tloc in range(NTH):
            gt = half * NTH + tloc
            p.op("scalar", lambda e, o=g_tm[:, gt * D:(gt + 1) * D],
                 i=g_acc[:, tloc * D:(tloc + 1) * D]: e.activation(o, i, AF.Silu),
                 waits=[("dve", max(gacc_rw.get((tloc, 0), 0), gacc_rw.get((tloc, 1), 0)))],
                 inc="act")

    def plan_C1(sweep):
        trbf = [psA[:, 0:512].bitcast(BF16), psB[:, 0:512].bitcast(BF16)]
        stage_free = [0, 0]
        ps_free = [0, 0]
        wt_done = {}
        for which, dst in (("wk", wkT), ("wv", wvT), ("wq", wqT)):
            last = 0
            for ec in range(4):
                sp = ec % 2
                wv_ = dma("gpsimd", wnatC[sp],
                          w_attn[which][sweep * 512 + ec * 128: sweep * 512 + (ec + 1) * 128, :],
                          f"wn{sp}", waits=[("pe", stage_free[sp])])
                tw = None
                for hc in range(8):
                    tw = p.op("tensor", lambda e, o=trbf[sp][:, hc * 128:(hc + 1) * 128],
                              i=wnatC[sp][:, hc * 128:(hc + 1) * 128], n=idb[:]:
                              e.transpose(o, i, n),
                              waits=[(f"wn{sp}", wv_), ("dve", ps_free[sp]), cws("idb")],
                              inc="pe")
                stage_free[sp] = tw
                cp = None
                for hc in range(8):
                    cp = p.op("vector",
                              lambda e, o=dst[:, hc * 512 + ec * 128: hc * 512 + (ec + 1) * 128],
                              i=trbf[sp][:, hc * 128:(hc + 1) * 128]: e.tensor_copy(o, i),
                              waits=[("pe", tw)], inc="dve")
                ps_free[sp] = cp
                last = cp
            wt_done[which] = last

        ones_done = 0
        for c in range(NB):
            ones_done = p.op("vector", lambda e, o=V_sb[:, c * NHS * 65:(c + 1) * NHS * 65]
                             .rearrange("p (h u) -> p h u", h=NHS)[:, :, 64:65]:
                             e.memset(o, 1.0), inc="dve")

        K_ps = [psA[:, 0:512], psA[:, 512:1024]]
        V_ps = [psB[:, 0:512], psB[:, 512:1024]]
        trx = [psA[:, 1024:1536].bitcast(BF16), psB[:, 1024:1536].bitcast(BF16)]
        Q_ps = [psC, psD]
        x_free = [0, 0]
        st_free = [0, 0]
        xnbf_free = [0, 0]
        trx_free = [0, 0]
        kps_free = [0, 0]
        vps_free = [0, 0]
        qps_free = [0, 0]
        xnc_free = [0, 0]
        grp_ready = {}
        xw_extra = [[("pe", stage_free[0])], [("pe", stage_free[1])]]

        def stage_stream(grp):
            gpar = grp % 2
            cp = None
            for ti in range(4):
                tk = grp * 4 + ti
                par = tk % 2
                xw = dma("sync", xtileC[par][:], xh[tk * 128:(tk + 1) * 128, :],
                         f"sx{par}", waits=[("dve", x_free[par])] + xw_extra[par])
                xw_extra[par] = []
                st = stC[par]
                c1 = p.op("vector", lambda e, a=st[:, 0:6], b=xtileC[par][:, 0:512]:
                          e.bn_stats(a, b), waits=[(f"sx{par}", xw), ("dve", st_free[par])],
                          inc="dve")
                c2 = p.op("vector", lambda e, a=st[:, 6:12], b=xtileC[par][:, 512:1024]:
                          e.bn_stats(a, b), waits=[("dve", c1)], inc="dve")
                c3 = p.op("vector", lambda e, a=st[:, 12:14], b=st[:, 0:12]: e.bn_aggr(a, b),
                          waits=[("dve", c2)], inc="dve")
                a1 = p.op("scalar", lambda e, a=st[:, 14:15], b=st[:, 13:14]:
                          e.activation(a, b, AF.Sqrt, bias=scc[:, 0:1], scale=1.0),
                          waits=[("dve", c3), cws("scc")], inc="act")
                c4 = p.op("vector", lambda e, a=st[:, 15:16], b=st[:, 14:15]:
                          e.reciprocal(a, b), waits=[("act", a1)], inc="dve")
                c5 = p.op("vector", lambda e, o=xnbfC[par][:], i=xtileC[par][:],
                          m=st[:, 12:13], r=st[:, 15:16]:
                          e.tensor_scalar(o, i, m, r, ALU.subtract, ALU.mult),
                          waits=[("dve", c4), ("pe", xnbf_free[par])], inc="dve")
                x_free[par] = c5
                st_free[par] = c5
                tw = None
                for hc in range(8):
                    tw = p.op("tensor", lambda e, o=trx[par][:, hc * 128:(hc + 1) * 128],
                              i=xnbfC[par][:, hc * 128:(hc + 1) * 128], n=idb[:]:
                              e.transpose(o, i, n),
                              waits=[("dve", c5), ("act", trx_free[par])], inc="pe")
                xnbf_free[par] = tw
                o3 = xnc[gpar].rearrange("p (c t) -> p c t", t=512)[:, :, ti * 128:
                                                                   (ti + 1) * 128]
                i3 = trx[par][:, 0:1024].rearrange("p (c t) -> p c t", t=128)
                cp = p.op("scalar", lambda e, o=o3, i=i3: e.copy(o, i),
                          waits=[("pe", tw), ("pe", xnc_free[gpar])], inc="act")
                trx_free[par] = cp
            grp_ready[grp] = cp

        def stage_mms(grp):
            gpar = grp % 2
            gr = grp_ready[grp]
            kv_last = 0
            for ec in range(4):
                kp = ec % 2
                mm = None
                for hc in range(8):
                    mm = p.op("tensor", lambda e, o=K_ps[kp],
                              l=wkT[:, hc * 512 + ec * 128: hc * 512 + (ec + 1) * 128],
                              r=xnc[gpar][:, hc * 512:(hc + 1) * 512],
                              s0=(hc == 0), s1=(hc == 7): e.matmul(o, l, r, start=s0, stop=s1),
                              waits=[("act", gr), ("dve", wt_done["wk"]),
                                     ("act", kps_free[kp])], inc="pe")
                kv_last = mm
                kps_free[kp] = p.op("scalar", lambda e,
                                    o=K_sb[:, ec * S + grp * 512:(ec * S + (grp + 1) * 512)],
                                    i=K_ps[kp]: e.activation(o, i, AF.Copy),
                                    waits=[("pe", mm)], inc="act")
            for ti in range(4):
                vp = ti % 2
                tk = grp * 4 + ti
                mm = None
                for hc in range(8):
                    mm = p.op("tensor", lambda e, o=V_ps[vp],
                              l=xnc[gpar][:, hc * 512 + ti * 128: hc * 512 + (ti + 1) * 128],
                              r=wvT[:, hc * 512:(hc + 1) * 512],
                              s0=(hc == 0), s1=(hc == 7): e.matmul(o, l, r, start=s0, stop=s1),
                              waits=[("act", gr), ("dve", wt_done["wv"]),
                                     ("act", vps_free[vp])], inc="pe")
                kv_last = mm
                vps_free[vp] = p.op("scalar", lambda e,
                                    o=V_sb[:, tk * NHS * 65:(tk + 1) * NHS * 65].rearrange(
                                        "p (h u) -> p h u", h=NHS)[:, :, 0:64],
                                    i=V_ps[vp].rearrange("p (h u) -> p h u", u=64),
                                    : e.activation(o, i, AF.Copy),
                                    waits=[("pe", mm)], inc="act")
            for ti in (0, 2):
                jt = 2 * grp + ti // 2
                for ec in range(4):
                    qp = ec % 2
                    mm = None
                    for hc in range(8):
                        mm = p.op("tensor", lambda e, o=Q_ps[qp][:, 0:128],
                                  l=wqT[:, hc * 512 + ec * 128: hc * 512 + (ec + 1) * 128],
                                  r=xnc[gpar][:, hc * 512 + ti * 128: hc * 512 + (ti + 1) * 128],
                                  s0=(hc == 0), s1=(hc == 7):
                                  e.matmul(o, l, r, start=s0, stop=s1),
                                  waits=[("act", gr), ("dve", wt_done["wq"]),
                                         ("act", qps_free[qp])], inc="pe")
                    kv_last = mm
                    qps_free[qp] = p.op("scalar", lambda e,
                                        o=Q_sb[:, ec * TOK + jt * 128: ec * TOK + (jt + 1) * 128],
                                        i=Q_ps[qp][:, 0:128]: e.activation(o, i, AF.Copy),
                                        waits=[("pe", mm)], inc="act")
            xnc_free[gpar] = kv_last

        NG = NB // 4
        for g in range(NG + 1):
            if g >= 1:
                stage_mms(g - 1)
            if g < NG:
                stage_stream(g)


    def plan_C2(sweep):
        NPQ = NP // 2
        sc_ps = [psA, psB]
        pv = psC
        bc = psD
        G = []
        for iq in range(NPQ):
            for h in range(NHS):
                chunks = list(range(8 * iq + 8))
                gs = [chunks[k:k + 3] for k in range(0, len(chunks), 3)]
                for gi, ch in enumerate(gs):
                    G.append((iq, h, ch, gi == 0, gi == len(gs) - 1))
        exp_done2 = [0, 0]
        probs_rd_pe = [0, 0, 0]
        probs_rd_dve = [0, 0, 0]
        araw_free = [0, 0]
        pv_reader_act = 0
        pv_reader_dve = 0
        bc_reader = 0
        qz_free = [0, 0]
        qz_ready = [0, 0]
        zinit = [0, 0]
        pend = []
        pend2 = []

        def head_slices(h):
            return h // 2, (h % 2) * 64

        for hp in range(2):
            zinit[hp] = p.op("vector",
                             lambda e, o=qz[hp][(1 - hp) * 64:(1 - hp) * 64 + 64, 0:512]:
                             e.memset(o, 0.0), inc="dve")
        mk_done = 0
        for r_ in range(1, 8):
            m_ = maskq[r_ - 1]
            mk_done = p.op("vector", lambda e, o=m_[:]: e.memset(o, 1.0),
                           waits=[("dve", mk_done)], inc="dve")
            for jl in range(4):
                rel = r_ - 2 * jl
                sl = (jl * 128, (jl + 1) * 128)
                if rel < 0:
                    continue
                elif rel == 0:
                    mk_done = p.op("vector", lambda e, o=m_[:, sl[0]:sl[1]], i=tri[:]:
                                   e.tensor_copy(o, i),
                                   waits=[("dve", mk_done), cws("tri")], inc="dve")
                elif rel == 1:
                    mk_done = p.op("vector", lambda e, o=m_[:, sl[0]:sl[1]],
                                   i=m_[:, sl[0]:sl[1]], s=flgc[:, jl:jl + 1]:
                                   e.tensor_single_scalar(o, i, s, ALU.mult),
                                   waits=[("dve", mk_done), cws("flg")], inc="dve")
                else:
                    mk_done = p.op("vector", lambda e, o=m_[:, sl[0]:sl[1]]:
                                   e.memset(o, 0.0), waits=[("dve", mk_done)], inc="dve")

        def plan_pv(item):
            nonlocal pv_reader_act, pv_reader_dve
            (t, iq, h, ch, first, last, par3, ed, md) = item
            pv_last = None
            for k, c in enumerate(ch):
                w = [("act", ed)]
                if md:
                    w.append(("dve", md))
                if first and k == 0:
                    w += [("dve", pv_reader_dve), ("act", pv_reader_act)]
                pv_last = p.op("tensor", lambda e, o=pv[0:65, 0:512],
                               l=V_sb[:, c * NHS * 65 + h * 65: c * NHS * 65 + (h + 1) * 65],
                               r=probs[par3][:, k * 512:(k + 1) * 512],
                               s0=(first and k == 0), s1=(last and k == len(ch) - 1):
                               e.matmul(o, l, r, start=s0, stop=s1), waits=w, inc="pe")
            probs_rd_pe[par3] = pv_last
            probs_rd_dve[par3] = md
            if last:
                apar = (iq * NHS + h) % 2
                rc = p.op("vector", lambda e, o=rsc[64:65, 0:512], i=pv[64:65, 0:512]:
                          e.reciprocal(o, i), waits=[("pe", pv_last)], inc="dve")
                acp = p.op("vector", lambda e, o=araw[apar][0:64, 0:512], i=pv[0:64, 0:512]:
                           e.tensor_copy(o, i),
                           waits=[("pe", pv_last), ("dve", araw_free[apar])], inc="dve")
                pv_reader_act = 0
                pv_reader_dve = acp

                def flush(iq=iq, h=h, rc=rc, acp=acp, apar=apar):
                    nonlocal bc_reader
                    ecq, b0q = head_slices(h)
                    bcw = p.op("tensor", lambda e, o=bc[0:64, 0:512],
                               l=ones[64:65, 0:64], r=rsc[64:65, 0:512]:
                               e.matmul(o, l, r, start=True, stop=True),
                               waits=[("dve", rc), cws("ones"), ("dve", bc_reader)],
                               inc="pe")
                    mb = p.op("vector", lambda e,
                              o=ao[sweep][b0q:b0q + 64,
                                          ecq * TOK + iq * 512: ecq * TOK + (iq + 1) * 512],
                              a=araw[apar][0:64, 0:512], b=bc[0:64, 0:512]:
                              e.tensor_mul(o, a, b),
                              waits=[("pe", bcw), ("dve", acp)], inc="dve")
                    bc_reader = mb
                    araw_free[apar] = mb
                pend.append(flush)

        prev = None
        prev2 = None
        last_qz = {}
        for t, (iq, h, ch, first, last) in enumerate(G):
            par = t % 2
            par3 = t % 3
            ec, b0 = head_slices(h)
            hp = h % 2
            if first:
                last_qz[(iq, h)] = p.op(
                    "vector", lambda e, o=qz[hp][b0:b0 + 64, 0:512],
                    i=Q_sb[b0:b0 + 64, ec * TOK + iq * 512: ec * TOK + (iq + 1) * 512]:
                    e.tensor_copy(o, i),
                    waits=[("pe", qz_free[hp]), ("dve", zinit[hp])], inc="dve")
                qz_ready[hp] = last_qz[(iq, h)]
            sc_last = None
            for k, c in enumerate(ch):
                sc_last = p.op("tensor", lambda e,
                               o=sc_ps[par][:, k * 512:(k + 1) * 512],
                               l=K_sb[:, ec * S + c * 128: ec * S + (c + 1) * 128],
                               r=qz[hp][:, 0:512]:
                               e.matmul(o, l, r, start=True, stop=True),
                               waits=[("act", exp_done2[par]), ("dve", qz_ready[hp])],
                               inc="pe")
            if last:
                qz_free[hp] = sc_last
            ed = p.op("scalar", lambda e, o=probs[par3][:, 0:len(ch) * 512],
                      i=sc_ps[par][:, 0:len(ch) * 512]:
                      e.activation(o, i, AF.Exp, bias=scc[:, 1:2], scale=QK_SCALE),
                      waits=[("pe", sc_last), cws("scc"),
                             ("pe", probs_rd_pe[par3]), ("dve", probs_rd_dve[par3])],
                      inc="act")
            exp_done2[par] = ed
            md = 0
            for k, c in enumerate(ch):
                r_ = c - 8 * iq
                if r_ < 0:
                    continue
                if r_ == 0:
                    md = p.op("vector", lambda e, o=probs[par3][:, k * 512: k * 512 + 128],
                              a=probs[par3][:, k * 512: k * 512 + 128], b=tri[:]:
                              e.tensor_mul(o, a, b),
                              waits=[("act", ed), cws("tri")], inc="dve")
                else:
                    md = p.op("vector", lambda e, o=probs[par3][:, k * 512:(k + 1) * 512],
                              a=probs[par3][:, k * 512:(k + 1) * 512], b=maskq[r_ - 1][:]:
                              e.tensor_mul(o, a, b),
                              waits=[("act", ed), ("dve", mk_done)], inc="dve")
            if prev2 is not None:
                plan_pv(prev2)
            for fn in pend2:
                fn()
            pend2 = pend
            pend = []
            prev2 = prev
            prev = (t, iq, h, ch, first, last, par3, ed, md)
        for item in (prev2, prev):
            if item is not None:
                plan_pv(item)
        for fn in pend2 + pend:
            fn()

    def plan_D():
        trbf = [psA[:, 0:512].bitcast(BF16), psB[:, 0:512].bitcast(BF16)]
        stage_free = [0, 0]
        ps_free = [0, 0]
        wo_done = 0
        for dc in range(8):
            sp = dc % 2
            wv_ = dma("gpsimd", wnatD[sp][:], w_attn["wo"][dc * 128:(dc + 1) * 128, :],
                      f"wn{sp}", waits=[("pe", stage_free[sp])])
            tw = None
            for ec in range(8):
                tw = p.op("tensor", lambda e, o=trbf[sp][:, (ec % 4) * 128:((ec % 4) + 1) * 128],
                          i=wnatD[sp][:, ec * 128:(ec + 1) * 128], n=idb[:]:
                          e.transpose(o, i, n),
                          waits=[(f"wn{sp}", wv_), ("dve", ps_free[sp]), cws("idb")], inc="pe")
                if ec % 4 == 3:
                    cp = None
                    for e4 in range(4):
                        ecg = ec - 3 + e4
                        cp = p.op("vector", lambda e,
                                  o=woT[:, (ecg) * D + dc * 128: (ecg) * D + (dc + 1) * 128],
                                  i=trbf[sp][:, e4 * 128:(e4 + 1) * 128]: e.tensor_copy(o, i),
                                  waits=[("pe", tw)], inc="dve")
                    ps_free[sp] = cp
                    wo_done = cp
            stage_free[sp] = tw
        s_ps = [psA[:, 512:1024], psB[:, 512:1024], psA[:, 0:512], psB[:, 0:512]]
        x_free = [0, 0]
        outdma = [0, 0]
        sp_free = [0, 0, 0, 0]
        outw = []
        pend = []
        for jt in range(NJ):
            par = jt % 2
            xw = dma("sync", xresD[par][:], xh[jt * 256: jt * 256 + 128, :],
                     f"sx{par}", waits=[("dve", x_free[par])])
            mm_of = {}
            for dh in range(2):
                pp = (jt % 2) * 2 + dh
                mm = None
                for ecg in range(8):
                    sweep, lc = ecg // 4, ecg % 4
                    mm = p.op("tensor", lambda e, o=s_ps[pp],
                              l=ao[sweep][:, lc * TOK + jt * 128: lc * TOK + (jt + 1) * 128],
                              r=woT[:, ecg * D + dh * 512: ecg * D + (dh + 1) * 512],
                              s0=(ecg == 0), s1=(ecg == 7): e.matmul(o, l, r, start=s0, stop=s1),
                              waits=[("dve", wo_done), ("dve", sp_free[pp])], inc="pe")
                mm_of[dh] = (pp, mm)
            for fn in pend:
                fn()
            pend = []

            def mk_ep(jt=jt, par=par, mm_of=mm_of, xw=xw):
                def do_ep():
                    lw = 0
                    for dh in range(2):
                        pp, mm = mm_of[dh]
                        t1 = p.op("vector", lambda e, o=outsb[par][:, dh * 512:(dh + 1) * 512],
                                  a=s_ps[pp], b=g_tm[:, jt * D + dh * 512: jt * D + (dh + 1) * 512]:
                                  e.tensor_mul(o, a, b),
                                  waits=[("pe", mm), (f"so{par}", outdma[par])], inc="dve")
                        lw = p.op("vector", lambda e, o=outsb[par][:, dh * 512:(dh + 1) * 512],
                                  a=outsb[par][:, dh * 512:(dh + 1) * 512],
                                  b=xresD[par][:, dh * 512:(dh + 1) * 512]:
                                  e.tensor_add(o, a, b),
                                  waits=[("dve", t1), (f"sx{par}", xw)], inc="dve")
                        sp_free[pp] = lw
                    x_free[par] = lw
                    ow = dma("sync", out_h[jt * 128:(jt + 1) * 128, :], outsb[par][:],
                             f"so{par}", waits=[("dve", lw)])
                    outdma[par] = ow
                    outw.append((par, ow, lw))
                return do_ep
            pend.append(mk_ep())
        for fn in pend:
            fn()
        return outw

    phases = [("A0", lambda: plan_A(0)), ("B0", lambda: plan_B(0)),
              ("A1t1", lambda: plan_A(1, 1)), ("A1t2", lambda: plan_A(1, 2)),
              ("A1t3", lambda: plan_A(1, 3)),
              ("A1", lambda: plan_A(1)), ("B1", lambda: plan_B(1)),
              ("C01", lambda: plan_C1(0)), ("C02", lambda: plan_C2(0)),
              ("C11", lambda: plan_C1(1)), ("C12", lambda: plan_C2(1)),
              ("D", lambda: plan_D())]
    for name, fn in phases:
        if name.startswith("A1t") and not upto.startswith("A1t"):
            continue
        fn()
        p.barrier()
        if name == upto:
            break
    p.op("sync", None, waits=[("so0", p.now("so0")), ("so1", p.now("so1"))])

    with nc.Block() as blk:
        p.emit(blk)
    return nc, es


def _layout(S=4096):
    NB = S // 128
    NJ = NB // 2
    cores = []
    for r in range(2):
        own, loc, flags = [], [], []
        for j in range(NJ):
            pj = (j % 2) if r == 0 else 1 - (j % 2)
            o = 2 * j + pj
            t = 2 * j + (1 - pj)
            own.append(o)
            loc += [o, t]
            flags.append(1.0 if pj == 1 else 0.0)
        cores.append((own, loc, flags))
    return cores


def _consts(gate_b, b1, gate_w):
    import ml_dtypes
    NEl = 8
    cf = np.zeros((128, 2384), np.float32)
    cf[:, 0:128] = np.eye(128, dtype=np.float32)
    cf[:, 128:192] = 1.0
    cf[:, 192:200] = np.broadcast_to(np.asarray(gate_b, np.float32)[None, :], (128, 8))
    cf[:, 200:264] = np.asarray(b1, np.float32).reshape(8, 8, 128)         .transpose(2, 0, 1).reshape(128, 64)
    cf[:, 264:328] = np.asarray(gate_w, np.float32).reshape(8, 128, 8)         .transpose(1, 0, 2).reshape(128, 64)
    cf[:, 328] = LN_EPS
    cf[:, 329] = EXP_SHIFT
    cf[0:8, 336:1360] = np.repeat(np.eye(8, dtype=np.float32)[:, :, None], 128, axis=2)         .reshape(8, 8 * 128)
    cb = np.zeros((128, 256), ml_dtypes.bfloat16)
    cb[:, 0:128] = np.eye(128, dtype=ml_dtypes.bfloat16)
    cb[:, 128:256] = np.tril(np.ones((128, 128), np.float32)).T.astype(ml_dtypes.bfloat16)
    return cf, cb


def kernel(L0_attn_x, L0_attn_wq, L0_attn_wk, L0_attn_wv, L0_attn_wo,
           L0_ffn_gate_w, L0_ffn_gate_b, L0_ffn_w1, L0_ffn_b1, L0_ffn_w2,
           L0_ffn_b2):
    x = np.ascontiguousarray(np.asarray(L0_attn_x, dtype=np.float32))
    B, S, _ = x.shape
    NJ = S // 256
    nc, es = build_program(S)
    cores = _layout(S)

    cf, cb = _consts(L0_ffn_gate_b, L0_ffn_b1, L0_ffn_gate_w)
    cf[0:8, 1360:2384] = np.asarray(L0_ffn_b2, np.float32)
    shared = {
        "wq": np.ascontiguousarray(np.asarray(L0_attn_wq, np.float32)),
        "wk": np.ascontiguousarray(np.asarray(L0_attn_wk, np.float32)),
        "wv": np.ascontiguousarray(np.asarray(L0_attn_wv, np.float32)),
        "wo": np.ascontiguousarray(np.asarray(L0_attn_wo, np.float32)),
        "w1": np.ascontiguousarray(np.asarray(L0_ffn_w1, np.float32)),
        "w2": np.ascontiguousarray(np.asarray(L0_ffn_w2, np.float32)),
        "consts_f32": cf,
        "consts_bf": cb,
    }
    in_maps = []
    metas = []
    for b in range(B):
        for r in range(2):
            own, loc, flags = cores[r]
            xb = x[b].reshape(S // 128, 128, 1024)
            x_core = np.ascontiguousarray(xb[loc].reshape(S, 1024))
            flags_col = np.ascontiguousarray(
                np.broadcast_to(np.asarray(flags, np.float32)[None, :], (128, NJ)))
            m = dict(shared)
            m["x_core"] = x_core
            m["flags_col"] = flags_col
            in_maps.append(m)
            metas.append((b, own))
    res = run_bass_kernel_spmd(nc, in_maps, core_ids=list(range(2 * B)))
    out = np.empty((B, S, 1024), np.float32)
    for ci, (b, own) in enumerate(metas):
        oc = np.asarray(res.results[ci]["out"]).reshape(NJ, 128, 1024)
        ob = out[b].reshape(S // 128, 128, 1024)
        for j, o in enumerate(own):
            ob[o] = oc[j]
    es.close()
    return out
